# version 14
# baseline (speedup 1.0000x reference)
import numpy as np
import ml_dtypes
import concourse.bass as bass
import concourse.mybir as mybir
from concourse.bass_utils import run_bass_kernel_spmd

F32 = mybir.dt.float32
BF16 = mybir.dt.bfloat16
BF = ml_dtypes.bfloat16
ALU = mybir.AluOpType
ACT = mybir.ActivationFunctionType
AX = mybir.AxisListType

S = 2048
E = 1024
NT = S // 128
NP = 8
NB = 2
NQT = S // 512

ENGINES = ("sync", "scalar", "vector", "tensor", "gpsimd")


class Prog:

    def __init__(self):
        self.ops = {e: [] for e in ENGINES}
        self.cnt = {}

    def emit(self, eng, fn, waits=(), inc=None, amt=1):
        waits = [w for w in waits if w is not None]
        self.ops[eng].append((tuple(waits), fn, inc, amt))
        if inc is not None:
            self.cnt[inc] = self.cnt.get(inc, 0) + amt
            return (inc, self.cnt[inc])
        return None

    def tick(self, sem):
        return (sem, self.cnt.get(sem, 0))

    def barrier(self):
        ticks = [self.tick(s) for s in list(self.cnt)]
        for e in ENGINES:
            if self.ops[e]:
                self.emit(e, lambda eng: eng.nop(), waits=ticks)

    def emitd(self, eng, fn, waits=(), inc=None, amt=1):
        return self.emit(eng, fn, waits=waits, inc=inc, amt=amt)

    def build(self, nc, block):
        from contextlib import ExitStack
        es = ExitStack()
        names = set(self.cnt)
        for ops in self.ops.values():
            for w, _, _, _ in ops:
                for sname, _v in w:
                    names.add(sname)
        sems = {n: es.enter_context(nc.semaphore(n)) for n in sorted(names)}

        def mk_body(engname):
            def body(eng):
                seen = {}
                for waits, fn, inc, amt in self.ops[engname]:
                    for sname, val in waits:
                        if val > 0 and seen.get(sname, 0) < val:
                            eng.wait_ge(sems[sname], val)
                            seen[sname] = val
                    ins = fn(eng)
                    if inc is not None:
                        assert ins is not None, f"no instruction returned on {engname}"
                        ins.then_inc(sems[inc], amt)
            return body

        for ename in ENGINES:
            if self.ops[ename]:
                getattr(block, ename)(mk_body(ename))
        return es


def rope_tables_np():
    D = 64
    inv_freq = (1.0 / (np.float32(10000.0) ** (np.arange(0, D, 2, dtype=np.float32) / np.float32(D)))).astype(np.float32)
    pos = np.arange(S, dtype=np.float32)
    ang = (pos[:, None] * inv_freq[None, :]).astype(np.float32)
    cos = np.cos(ang).astype(np.float32)
    sin = np.sin(ang).astype(np.float32)
    cos_t = np.empty((128, S), np.float32)
    sin_t = np.empty((128, S), np.float32)
    for d in range(128):
        j = d % 64
        if j < 32:
            cos_t[d] = cos[:, j]
            sin_t[d] = -sin[:, j]
        else:
            cos_t[d] = cos[:, j - 32]
            sin_t[d] = sin[:, j - 32]
    return cos_t, sin_t


def hadamard_np():
    h = np.array([[1.0]], dtype=np.float64)
    while h.shape[0] < E:
        h = np.block([[h, h], [h, -h]])
    return (h / np.sqrt(E)).astype(np.float32)


def host_consts():
    cos_t, sin_t = rope_tables_np()
    had = hadamard_np().astype(BF)
    iden = np.eye(128, dtype=np.float32).astype(BF)
    perm = np.zeros((128, 128), np.float32)
    for m in range(128):
        base = (m // 64) * 64
        j = m % 64
        perm[base + (j + 32) % 64, m] = 1.0
    perm = perm.astype(BF)
    perm2 = np.zeros((128, 128), np.float32)
    for m in range(128):
        perm2[(m + 64) % 128, m] = 1.0
    perm2 = perm2.astype(BF)
    sel = np.zeros((128, 128), np.float32)
    sel[0, :64] = 1.0
    sel[32, 64:] = 1.0
    sel = sel.astype(BF)
    ltri = (np.arange(128)[None, :] >= np.arange(128)[:, None]).astype(np.float32).astype(BF)
    ones512 = np.ones((1, 512), np.float32).astype(BF)
    onescol = np.ones((128, 1), np.float32).astype(BF)
    epsrow = np.full((1, 128), 1e-30, np.float32).astype(BF)
    return dict(cos_t=cos_t, sin_t=sin_t, had=had, iden=iden, perm=perm,
                perm2=perm2, sel=sel, ltri=ltri, ones512=ones512, onescol=onescol,
                epsrow=epsrow)


def build_nc(dbg=False):
    nc = bass.Bass("TRN2", target_bir_lowering=False)
    P = Prog()

    x_in = nc.dram_tensor("x_in", [NB, S, E], F32, kind="ExternalInput").ap()
    wq_w = nc.dram_tensor("wq_w", [E, E], F32, kind="ExternalInput").ap()
    wkv_w = nc.dram_tensor("wkv_w", [E, 2 * E], F32, kind="ExternalInput").ap()
    wo_w = nc.dram_tensor("wo_w", [E, E], F32, kind="ExternalInput").ap()
    wf_w = nc.dram_tensor("wf_w", [E, E], F32, kind="ExternalInput").ap()
    bq_t = nc.dram_tensor("bq_t", [128, 8], F32, kind="ExternalInput").ap()
    bk_t = nc.dram_tensor("bk_t", [128, 8], F32, kind="ExternalInput").ap()
    bv_row = nc.dram_tensor("bv_row", [1, E], F32, kind="ExternalInput").ap()
    bo_row = nc.dram_tensor("bo_row", [1, E], F32, kind="ExternalInput").ap()
    cos_h = nc.dram_tensor("cos_t", [128, S], F32, kind="ExternalInput").ap()
    sin_h = nc.dram_tensor("sin_t", [128, S], F32, kind="ExternalInput").ap()
    had_h = nc.dram_tensor("had", [E, E], BF16, kind="ExternalInput").ap()
    iden_h = nc.dram_tensor("iden", [128, 128], BF16, kind="ExternalInput").ap()
    perm_h = nc.dram_tensor("perm", [128, 128], BF16, kind="ExternalInput").ap()
    perm2_h = nc.dram_tensor("perm2", [128, 128], BF16, kind="ExternalInput").ap()
    sel_h = nc.dram_tensor("sel", [128, 128], BF16, kind="ExternalInput").ap()
    ltri_h = nc.dram_tensor("ltri", [128, 128], BF16, kind="ExternalInput").ap()
    ones_h = nc.dram_tensor("ones512", [1, 512], BF16, kind="ExternalInput").ap()
    onescol_h = nc.dram_tensor("onescol", [128, 1], BF16, kind="ExternalInput").ap()
    eps_h = nc.dram_tensor("epsrow", [1, 128], BF16, kind="ExternalInput").ap()
    out_h = nc.dram_tensor("out", [NB, S, E], F32, kind="ExternalOutput").ap()
    wqT_d = nc.dram_tensor("wqT_d", [128, 8 * E], BF16, kind="Internal").ap()
    dbg_h = {}
    if dbg:
        for name, shape, dt in [("hT", [128, NP * S], BF16), ("QT", [128, NP * S], BF16),
                                ("KT", [128, NP * S], BF16), ("Vt", [128, NT * E], BF16),
                                ("OT", [128, NP * S], BF16), ("x2", [S, E], F32),
                                ("wqT", [128, 8 * E], BF16)]:
            dbg_h[name] = nc.dram_tensor("dbg_" + name, shape, dt, kind="ExternalOutput").ap()

    A = nc.alloc_sbuf_tensor("A_qt", [128, NP * S], BF16)
    B = nc.alloc_sbuf_tensor("B_kt", [128, NP * S], BF16)
    C = nc.alloc_sbuf_tensor("C_v", [128, NT * E], BF16)
    D = nc.alloc_sbuf_tensor("D_ht", [128, NP * S], BF16)
    Ew = nc.alloc_sbuf_tensor("E_w", [128, 8 * E], BF16)
    tabbuf = nc.alloc_sbuf_tensor("tabbuf", [128, 1024], F32)
    Fq = nc.alloc_sbuf_tensor("F_q", [128, 2 * E], F32)
    bbufq = nc.alloc_sbuf_tensor("bbufq", [128, E], BF16)
    G = nc.alloc_sbuf_tensor("G_pt", [128, 2048], BF16)
    KBs = nc.alloc_sbuf_tensor("KB_s", [128, S], BF16)
    QBs = nc.alloc_sbuf_tensor("QB_s", [128, 2 * 512], BF16)
    QAs = nc.alloc_sbuf_tensor("QA_s", [128, 2 * 512], BF16)
    Wstr = nc.alloc_sbuf_tensor("W_str", [128, 2 * E], BF16)
    xbuf = nc.alloc_sbuf_tensor("xbuf", [128, 2 * E], F32)
    x2buf = nc.alloc_sbuf_tensor("x2buf", [128, 2 * E], F32)
    hbuf = nc.alloc_sbuf_tensor("hbuf", [128, 2 * E], BF16)
    qibuf = nc.alloc_sbuf_tensor("qibuf", [128, E], BF16)
    qiT = nc.alloc_sbuf_tensor("qiT", [128, E], BF16)
    h2T = qiT
    wqtile = qibuf
    recip_sb = nc.alloc_sbuf_tensor("recip_sb", [128, 512], BF16)
    stats = nc.alloc_sbuf_tensor("stats", [128, 32], F32)
    consts = nc.alloc_sbuf_tensor("consts", [128, 5 * 128], BF16)
    onesc = nc.alloc_sbuf_tensor("onesc", [128, 1], BF16)
    bqs = nc.alloc_sbuf_tensor("bqs", [128, 8], F32)
    bks = nc.alloc_sbuf_tensor("bks", [128, 8], F32)
    bvr_t = nc.alloc_sbuf_tensor("bvr_t", [1, E], BF16)
    bor_t = nc.alloc_sbuf_tensor("bor_t", [1, E], BF16)
    onesr_t = nc.alloc_sbuf_tensor("onesr_t", [1, 512], BF16)
    epsr_t = nc.alloc_sbuf_tensor("epsr_t", [1, 128], BF16)
    bvr = bvr_t[0:1, 0:E]
    bor = bor_t[0:1, 0:E]
    onesr = onesr_t[0:1, 0:512]
    epsr = epsr_t[0:1, 0:128]
    zror = epsr

    iden = consts[:, 0:128]
    perm = consts[:, 128:256]
    perm2 = consts[:, 256:384]
    sel = consts[:, 384:512]
    ltri = consts[:, 512:640]
    COSs = tabbuf[:, 0:512]
    SINs = tabbuf[:, 512:1024]
    obuf = tabbuf[:, 0:E]
    ropeb = Fq[:, 0:1024]
    lnscr = Fq[:, 0:512]
    Rcopy = Fq[:, 512:1024]
    wf32 = Fq[:, 0:E]

    PS = nc.alloc_psum_tensor("PS", [128, 4096], F32)

    def bank(i, lo=0, hi=512, p0=0, p1=128):
        return PS[p0:p1, i * 512 + lo:i * 512 + hi]

    tokb = [None] * 8

    def mm(lhsT, rhs, out, start, stop, waits=(), tp=None):
        def f(t, lhsT=lhsT, rhs=rhs, out=out, start=start, stop=stop, tp=tp):
            return t.matmul(out, lhsT, rhs, start=start, stop=stop, tile_position=tp)
        P.emit("tensor", f, waits=waits, inc="tT")
        return ("tT", P.cnt["tT"])

    cl = []
    for dst, src in [(consts[:, 0:128], iden_h), (consts[:, 128:256], perm_h),
                     (consts[:, 256:384], perm2_h), (consts[:, 384:512], sel_h),
                     (consts[:, 512:640], ltri_h),
                     (onesr, ones_h), (onesc[:], onescol_h), (epsr, eps_h),
                     (bqs[:], bq_t), (bks[:], bk_t),
                     ]:
        cl.append(P.emit("sync", (lambda d, s: (lambda e: e.dma_start(out=d, in_=s)))(dst, src),
                         inc="dS", amt=16))
    cl.append(P.emit("gpsimd", lambda g: g.dma_start(out=bvr, in_=bv_row), inc="dG", amt=16))
    cl.append(P.emit("gpsimd", lambda g: g.dma_start(out=bor, in_=bo_row), inc="dG", amt=16))
    eps_tok = P.emit("vector", lambda v: v.memset(stats[:, 30:31], 1e-5), inc="tV")
    m8_tok = P.emit("vector", lambda v: v.memset(stats[:, 29:30], -8.0), inc="tV")
    cl.append(m8_tok)
    P.emit("vector", lambda v: v.memset(QBs[:], 0.0))
    P.emit("vector", lambda v: v.memset(QAs[:], 0.0))
    zpad_tok = P.emit("vector", lambda v: v.memset(KBs[:], 0.0), inc="tV")
    cl.append(zpad_tok)
    cl.append(eps_tok)
    consts_tok = cl[:]

    out_final = []

    for b in range(NB):
        xb = x_in[b]
        ob = out_h[b]
        prev_done = [P.tick("tT"), P.tick("tV"), P.tick("tS")]

        P.barrier()
        wstr_last = {}
        rope_cons = {}
        wv_toks = []
        for e in range(8):
            wv_toks.append(P.emit(
                "gpsimd",
                (lambda e=e: (lambda g: g.dma_start(
                    out=Ew[:, e * E:(e + 1) * E],
                    in_=wkv_w[e * 128:(e + 1) * 128, E:2 * E])))(),
                waits=prev_done, inc="dG", amt=16))

        httok = {}
        hnorm = {}
        xcons = {}
        hmm_t = {}

        for tt in range(NT):
            sl = tt % 2
            xa = xbuf[:, sl * E:(sl + 1) * E]
            ha = hbuf[:, sl * E:(sl + 1) * E]
            xl = P.emit("sync",
                        (lambda tt=tt, xa=xa, xb=xb: (lambda e: e.dma_start(
                            out=xa, in_=xb[tt * 128:(tt + 1) * 128, :])))(),
                        waits=[xcons.get(tt - 2)] + (prev_done if tt < 2 else []),
                        inc="dS", amt=16)
            P.emit("vector", (lambda xa=xa: (lambda v: v.bn_stats(stats[:, 0:6], xa[:, 0:512])))(), waits=[xl])
            P.emit("vector", (lambda xa=xa: (lambda v: v.bn_stats(stats[:, 6:12], xa[:, 512:1024])))())
            P.emit("vector", lambda v: v.drain())
            P.emit("vector", lambda v: v.bn_aggr(stats[:, 12:14], stats[:, 0:12]))
            P.emit("vector", lambda v: v.drain())
            P.emit("vector", lambda v: v.tensor_scalar_mul(stats[:, 14:15], stats[:, 12:13], -1.0))
            vt = P.emit("vector", lambda v: v.tensor_copy(stats[:, 15:16], stats[:, 13:14]), inc="tV")
            st = P.emit("scalar", lambda s: s.activation(stats[:, 16:17], stats[:, 15:16], ACT.Sqrt,
                                                         bias=stats[:, 30:31]),
                        waits=[vt, eps_tok], inc="tS")
            P.emit("vector", lambda v: v.reciprocal(stats[:, 17:18], stats[:, 16:17]), waits=[st])
            P.emit("vector", lambda v: v.drain())
            hnorm[tt] = P.emitd("vector",
                               (lambda xa=xa, ha=ha: (lambda v: v.tensor_scalar(
                                   ha, xa, stats[:, 14:15], stats[:, 17:18], ALU.add, ALU.mult)))(),
                               waits=[hmm_t.get(tt - 2)], inc="tV")
            xcons[tt] = hnorm[tt]
            httok[tt] = []
            for half in range(2):
                bk = half
                w0 = [hnorm[tt], tokb[bk], consts_tok[0]] + (prev_done if tt < 2 else [])
                mt = None
                for j in range(4):
                    e = half * 4 + j
                    mt = mm(ha[:, e * 128:(e + 1) * 128], iden,
                            bank(bk, j * 128, (j + 1) * 128),
                            start=True, stop=True, waits=w0 if j == 0 else ())
                dst = D[:].rearrange("p (c t) -> p c t", c=NP)[:, half * 4:(half + 1) * 4,
                                                              tt * 128:(tt + 1) * 128]
                src = bank(bk).rearrange("p (c t) -> p c t", c=4)
                tokb[bk] = P.emitd("vector",
                                  (lambda dst=dst, src=src: (lambda v: v.tensor_copy(dst, src)))(),
                                  waits=[mt], inc="tV")
                httok[tt].append(tokb[bk])
            hmm_t[tt] = P.tick("tT")
            for half in range(2):
                bk = 2 + half
                w0 = [tokb[bk], wv_toks[-1], consts_tok[4]] + httok[tt]
                mm(onesr[0:1, 0:128], bvr[0:1, half * 512:(half + 1) * 512], bank(bk),
                   start=True, stop=False, waits=w0)
                mt = None
                for e in range(8):
                    mt = mm(D[:, e * S + tt * 128: e * S + (tt + 1) * 128],
                            Ew[:, e * E + half * 512: e * E + (half + 1) * 512], bank(bk),
                            start=False, stop=(e == 7))
                tokb[bk] = P.emitd("vector",
                                  (lambda bk=bk, tt=tt, half=half: (lambda v: v.tensor_copy(
                                      C[:, tt * E + half * 512: tt * E + (half + 1) * 512], bank(bk))))(),
                                  waits=[mt], inc="tV")

            if tt % 4 == 3:
                tg = tt // 4
                tabtok = P.emit("sync",
                                (lambda tg=tg: (lambda e: e.dma_start(
                                    out=COSs, in_=cos_h[:, tg * 512:(tg + 1) * 512])))(),
                                waits=[rope_cons.get(tg - 1)], inc="dS", amt=16)
                tabtok2 = P.emit("sync",
                                 (lambda tg=tg: (lambda e: e.dma_start(
                                     out=SINs, in_=sin_h[:, tg * 512:(tg + 1) * 512])))(),
                                 inc="dS", amt=16)
                def emit_rope(dslice, ev, tg=tg, tabtok=tabtok, tabtok2=tabtok2):
                    mt2 = mm(perm, dslice, bank(6), start=True, stop=True,
                             waits=[tokb[6], ev])
                    p1 = P.emit("vector",
                                lambda v: v.tensor_mul(ropeb[:, 0:512], bank(6), SINs),
                                waits=[mt2, tabtok, tabtok2], inc="tV")
                    tokb[6] = p1
                    P.emit("vector",
                           (lambda dslice=dslice: (lambda v: v.tensor_mul(
                               ropeb[:, 512:1024], dslice, COSs)))(),
                           inc="tV")
                    rope_cons[tg] = P.emitd("vector",
                                            (lambda dslice=dslice: (lambda v: v.tensor_add(
                                                dslice, ropeb[:, 0:512], ropeb[:, 512:1024])))(),
                                            inc="tV")

                rope_pend = None
                for ti, (Wsrc, dstT, bias_t) in enumerate([(wq_w, A, bqs), (wkv_w, B, bks)]):
                    for p in range(NP):
                        ssl = p % 2
                        wtok = P.emit(
                            "gpsimd",
                            (lambda Wsrc=Wsrc, p=p, ssl=ssl: (lambda g: g.dma_start(
                                out=Wstr[:, ssl * E:(ssl + 1) * E].rearrange("q (c t) -> q c t", c=8),
                                in_=Wsrc.rearrange("(c q) d -> q c d", q=128)[:, :, p * 128:(p + 1) * 128])))(),
                            waits=[wstr_last.get(ssl)], inc="dG", amt=16)
                        bk = 4 + (p % 2)
                        w0 = [tokb[bk], wtok] + httok[tt]
                        mt = None
                        for e in range(8):
                            mt = mm(Wstr[:, ssl * E + e * 128: ssl * E + (e + 1) * 128],
                                    D[:, e * S + tg * 512: e * S + (tg + 1) * 512], bank(bk),
                                    start=(e == 0), stop=(e == 7), waits=w0 if e == 0 else ())
                        wstr_last[ssl] = P.tick("tT")
                        dslice = dstT[:, p * S + tg * 512: p * S + (tg + 1) * 512]
                        ev = P.emitd(
                            "scalar",
                            (lambda bk=bk, dslice=dslice, bias_t=bias_t, p=p: (lambda s: s.activation(
                                dslice, bank(bk), ACT.Identity, bias=bias_t[:, p:p + 1])))(),
                            waits=[mt], inc="tS")
                        tokb[bk] = ev
                        if rope_pend is not None:
                            emit_rope(*rope_pend)
                        rope_pend = (dslice, ev)
                if rope_pend is not None:
                    emit_rope(*rope_pend)
                    rope_pend = None

        rope_done = [P.tick("tV")]
        if dbg and b == 0:
            for name, src in [("hT", D[:]), ("QT", A[:]), ("KT", B[:]), ("Vt", C[:])]:
                out_final.append(P.emit("sync",
                                        (lambda name=name, src=src: (lambda e: e.dma_start(
                                            out=dbg_h[name], in_=src)))(),
                                        waits=rope_done + [P.tick("tT"), P.tick("tS")],
                                        inc="dO", amt=16))

        P.barrier()
        ot_last = None
        pending = None

        def emit_closing(hp, qt, sbk):
            nonlocal ot_last
            smt = mm(epsr[0:1, :], onesr[0:1, :], bank(sbk), start=False, stop=True)
            l1 = P.emit("scalar", (lambda sbk=sbk: (lambda s: s.activation(
                lnscr[0:33, :], bank(sbk, p0=0, p1=33), ACT.Ln)))(),
                waits=[smt], inc="tS")
            l2 = P.emit("scalar", lambda s: s.activation(recip_sb[0:33, :], lnscr[0:33, :],
                                                         ACT.Exp, scale=-1.0),
                        waits=[l1], inc="tS")
            tokb[sbk] = l2
            rt = mm(sel[0:33, :], recip_sb[0:33, :], bank(4), start=True, stop=True,
                    waits=[tokb[4], l2])
            rc = P.emit("vector", lambda v: v.tensor_copy(Rcopy, bank(4)),
                        waits=[rt], inc="tV")
            tokb[4] = rc
            dsl = D[:, hp * S + qt * 512: hp * S + (qt + 1) * 512]
            ot = P.emit("vector",
                        (lambda dsl=dsl: (lambda v: v.tensor_mul(dsl, bank(2), Rcopy)))(),
                        waits=[rc], inc="tV")
            tokb[2] = ot
            ot_last = ot

        for hp in range(NP):
            kb_toks = []
            kb_banks = (0, 1, 6, 7)
            for j in range(4):
                bkj = kb_banks[j]
                mm(perm2, B[:, hp * S + j * 512: hp * S + (j + 1) * 512], bank(bkj),
                   start=True, stop=True, waits=[tokb[bkj]] + rope_done)
            kmt = P.tick("tT")
            for half in range(2):
                pbase = 0 if half == 0 else 3072
                tk = P.emit("vector",
                            (lambda half=half, pbase=pbase: (lambda v: v.tensor_copy(
                                KBs[0:64, half * 1024:(half + 1) * 1024],
                                PS[0:64, pbase:pbase + 1024])))(),
                            waits=[kmt], inc="tV")
                tokb[kb_banks[2 * half]] = tk
                tokb[kb_banks[2 * half + 1]] = tk
                kb_toks.append(tk)
            def stage_q(qt, hp=hp):
                qp = qt % 2
                mt = mm(perm2, A[:, hp * S + qt * 512: hp * S + (qt + 1) * 512], bank(4),
                        start=True, stop=True, waits=[tokb[4], q_cons.get(qp)] + rope_done)
                qb_tok = P.emit("vector",
                                (lambda qp=qp: (lambda v: v.tensor_copy(
                                    QBs[0:64, qp * 512:(qp + 1) * 512], bank(4, p0=0, p1=64))))(),
                                waits=[mt], inc="tV")
                tokb[4] = qb_tok
                qa_tok = P.emit("vector",
                                (lambda hp=hp, qt=qt, qp=qp: (lambda v: v.tensor_copy(
                                    QAs[0:64, qp * 512:(qp + 1) * 512],
                                    A[0:64, hp * S + qt * 512: hp * S + (qt + 1) * 512])))(),
                                waits=rope_done + [mt, q_cons.get(qp)], inc="tV")
                q_toks[qp] = (qa_tok, qb_tok)

            q_cons = {}
            q_toks = {}
            stage_q(0)
            for qt in range(NQT):
                nk = 4 * (qt + 1)
                qp = qt % 2
                sbk = 3 if qt % 2 == 0 else 5
                qa_tok, qb_tok = q_toks[qp]
                pt_tok = [None, None]
                s_tok = {}

                def emit_S(kt, hp=hp, qt=qt, qp=qp, kb_last=kb_toks[-1], qb_tok=qb_tok,
                           qa_tok=qa_tok, s_tok=s_tok):
                    bA = 0 if kt % 2 == 0 else 6
                    bB = bA + 1
                    mm(B[:, hp * S + kt * 128: hp * S + (kt + 1) * 128],
                       QAs[:, qp * 512:(qp + 1) * 512], bank(bA),
                       start=True, stop=True, waits=[tokb[bA], qa_tok, zpad_tok] + rope_done)
                    mm(KBs[:, kt * 128:(kt + 1) * 128], QBs[:, qp * 512:(qp + 1) * 512], bank(bB),
                       start=True, stop=True, waits=[tokb[bB], kb_last, qb_tok])
                    s_tok[kt] = P.tick("tT")

                emit_S(0)
                if nk > 1:
                    emit_S(1)
                mm(epsr[0:1, :], onesr[0:1, :], bank(sbk), start=True, stop=False,
                   waits=[tokb[sbk]])
                if pending is not None:
                    emit_closing(*pending)
                    pending = None
                for kt in range(nk):
                    par = kt % 2
                    bA = 0 if par == 0 else 6
                    pt_a = G[:, par * 1024: par * 1024 + 512]
                    pt_b = G[:, par * 1024 + 512: par * 1024 + 1024]
                    joff = max(0, kt * 128 - qt * 512)
                    if joff:
                        def expfn(s, par=par, bA=bA, joff=joff):
                            gi = G[:, par * 1024:(par + 1) * 1024].rearrange(
                                "p (h q) -> p h q", h=2)[:, :, joff:512]
                            pi = PS[:, bA * 512:(bA + 2) * 512].rearrange(
                                "p (h q) -> p h q", h=2)[:, :, joff:512]
                            return s.activation(gi, pi, ACT.Exp, scale=0.125)
                    else:
                        def expfn(s, par=par, bA=bA):
                            return s.activation(G[:, par * 1024:(par + 1) * 1024],
                                                PS[:, bA * 512:(bA + 2) * 512],
                                                ACT.Exp, scale=0.125)
                    etok = P.emitd("scalar", expfn,
                                  waits=[s_tok[kt], pt_tok[par]], inc="tS")
                    tokb[bA] = etok
                    tokb[bA + 1] = etok
                    ptok = etok
                    if kt >= 4 * qt:
                        joff = kt * 128 - qt * 512
                        slot = G[:, par * 1024:(par + 1) * 1024].rearrange("p (h q) -> p h q", h=2)
                        if joff:
                            P.emit("vector",
                                   (lambda slot=slot, joff=joff: (lambda v: v.memset(
                                       slot[:, :, 0:joff], 0.0)))(),
                                   waits=[etok], inc="tV")
                        ptok = P.emit("vector",
                                      (lambda slot=slot, joff=joff: (lambda v: v.tensor_mul(
                                          slot[:, :, joff:joff + 128], slot[:, :, joff:joff + 128],
                                          ltri.rearrange("p (a q) -> p a q", a=1).broadcast_to([128, 2, 128]))))(),
                                      waits=[etok], inc="tV")
                    if kt + 2 < nk:
                        emit_S(kt + 2)
                    pvw = [ptok] + ([tokb[2]] if kt == 0 else [])
                    mm(C[:, kt * E + (2 * hp) * 64: kt * E + (2 * hp) * 64 + 64], pt_a,
                       bank(2, p0=0, p1=64), start=(kt == 0), stop=(kt == nk - 1),
                       waits=pvw, tp=(0, 0))
                    mm(C[:, kt * E + (2 * hp + 1) * 64: kt * E + (2 * hp + 1) * 64 + 64], pt_b,
                       bank(2, p0=64, p1=128), start=(kt == 0), stop=(kt == nk - 1),
                       tp=(0, 64))
                    mm(onesc[:], pt_a, bank(sbk, p0=0, p1=1), start=False, stop=False, tp=(0, 0))
                    pt_tok[par] = mm(onesc[:], pt_b, bank(sbk, p0=32, p1=33), start=False,
                                     stop=False, tp=(0, 32))
                    if kt == 0 and qt + 1 < NQT:
                        stage_q(qt + 1)
                q_cons[qp] = s_tok[nk - 1]
                pending = (hp, qt, sbk)
        emit_closing(*pending)
        pending = None

        if dbg and b == 0:
            out_final.append(P.emit("sync", lambda e: e.dma_start(out=dbg_h["OT"], in_=D[:]),
                                    waits=[ot_last], inc="dO", amt=16))

        P.barrier()
        attn_done = [ot_last]
        h_toks = []
        for e in range(8):
            h_toks.append(P.emit("sync",
                                 (lambda e=e: (lambda s: s.dma_start(
                                     out=C[:, e * E:(e + 1) * E],
                                     in_=had_h[e * 128:(e + 1) * 128, :])))(),
                                 waits=attn_done + [P.tick("tT")], inc="dS", amt=16))
        wo_toks = []
        for e in range(8):
            wo_toks.append(P.emit("gpsimd",
                                  (lambda e=e: (lambda g: g.dma_start(
                                      out=Ew[:, e * E:(e + 1) * E],
                                      in_=wo_w[e * 128:(e + 1) * 128, :])))(),
                                  waits=attn_done + [P.tick("tT")], inc="dG", amt=16))
        wq_tok = None
        if b == 1:
            wq_tok = P.emit("sync",
                            lambda e: e.dma_start(out=C[:, 8 * E:16 * E], in_=wqT_d),
                            waits=attn_done + [P.tick("tT")], inc="dS", amt=16)
        for oc in range(8 if b == 0 else 0):
            lt = P.emit("sync",
                        (lambda oc=oc: (lambda s: s.dma_start(
                            out=Fq[:, 0:E], in_=wf_w[oc * 128:(oc + 1) * 128, :])))(),
                        waits=attn_done + [P.tick("tT"), wq_tok], inc="dS", amt=16)
            P.emit("vector", lambda v: v.reduce_sum(stats[:, 25:26], Fq[:, 0:E], AX.X,
                                                    apply_absolute_value=True), waits=[lt])
            P.emit("vector", lambda v: v.drain())
            P.emit("vector", lambda v: v.tensor_scalar(stats[:, 26:27], stats[:, 25:26],
                                                       1.0 / 1024.0, 1e-5, ALU.mult, ALU.max))
            P.emit("vector", lambda v: v.drain())
            P.emit("vector", lambda v: v.reciprocal(stats[:, 27:28], stats[:, 26:27]))
            P.emit("vector", lambda v: v.drain())
            P.emit("vector", lambda v: v.tensor_scalar(Fq[:, E:2 * E], Fq[:, 0:E], stats[:, 27:28], None, ALU.mult))
            P.emit("vector", lambda v: v.tensor_single_scalar(bbufq[:], Fq[:, E:2 * E], 0.5, ALU.is_gt))
            P.emit("vector", lambda v: v.tensor_single_scalar(Fq[:, 0:E], Fq[:, E:2 * E], -0.5, ALU.is_lt))
            P.emit("vector", lambda v: v.tensor_sub(Fq[:, E:2 * E], bbufq[:], Fq[:, 0:E]))
            qt_ = P.emitd("vector", lambda v: v.tensor_scalar(wqtile[:], Fq[:, E:2 * E], stats[:, 26:27],
                                                             None, ALU.mult), inc="tV")
            for half in range(2):
                bk = half
                w0 = [qt_, tokb[bk]]
                mt = None
                for j in range(4):
                    ic = half * 4 + j
                    mt = mm(wqtile[:, ic * 128:(ic + 1) * 128], iden,
                            bank(bk, j * 128, (j + 1) * 128), start=True, stop=True,
                            waits=w0 if j == 0 else ())
                dst = C[:].rearrange("p (c t) -> p c t", c=2 * NP)[:, 8 + half * 4: 8 + (half + 1) * 4,
                                                                  oc * 128:(oc + 1) * 128]
                src = bank(bk).rearrange("p (c t) -> p c t", c=4)
                tokb[bk] = P.emitd("vector",
                                  (lambda dst=dst, src=src: (lambda v: v.tensor_copy(dst, src)))(),
                                  waits=[mt], inc="tV")
                wq_tok = tokb[bk]
        if b == 0:
            wq_tok = P.emit("sync",
                            lambda e: e.dma_start(out=wqT_d, in_=C[:, 8 * E:16 * E]),
                            waits=[wq_tok], inc="dS", amt=16)
        wprep = [wq_tok, h_toks[-1], wo_toks[-1]]
        if dbg and b == 0:
            out_final.append(P.emit("sync", lambda e: e.dma_start(out=dbg_h["wqT"], in_=C[:, 8 * E:16 * E]),
                                    waits=[wq_tok], inc="dO", amt=16))

        P.barrier()
        x2cons = {}
        x2free = {}
        ostore = {}
        x2toks = {}
        xh_toks = {}
        h2T_toks = {}
        ffn_mts = {}
        tp_toks = {}

        def cd_stage1(tt):
            sl = tt % 2
            xa = xbuf[:, sl * E:(sl + 1) * E]
            xl = P.emit("sync",
                        (lambda tt=tt, xa=xa, xb=xb: (lambda e: e.dma_start(
                            out=xa, in_=xb[tt * 128:(tt + 1) * 128, :])))(),
                        waits=[x2cons.get(tt - 2, hnorm[NT - 2 + sl])], inc="dS", amt=16)
            for half in range(2):
                bk = half
                w0 = [tokb[bk], attn_done[0], x2free.get(tt - 2)] + wprep
                mm(onesr[0:1, 0:128], bor[0:1, half * 512:(half + 1) * 512], bank(bk),
                   start=True, stop=False, waits=w0)
                mt = None
                for vc in range(8):
                    mt = mm(D[:, vc * S + tt * 128: vc * S + (tt + 1) * 128],
                            Ew[:, vc * E + half * 512: vc * E + (half + 1) * 512], bank(bk),
                            start=False, stop=(vc == 7))
                tokb[bk] = P.emit("vector",
                                  (lambda bk=bk, sl=sl, half=half: (lambda v: v.tensor_add(
                                      x2buf[:, sl * E + half * 512: sl * E + (half + 1) * 512],
                                      bank(bk),
                                      xbuf[:, sl * E + half * 512: sl * E + (half + 1) * 512])))(),
                                  waits=[mt, xl, x2free.get(tt - 2)], inc="tV")
            x2cons[tt] = tokb[1]
            x2toks[tt] = tokb[1]

        def cd_stage2(tt):
            sl = tt % 2
            h2Tp = h2T[:] if sl == 0 else G[:, 0:1024]
            x2tok = x2toks[tt]
            x2a = x2buf[:, sl * E:(sl + 1) * E]
            if dbg and b == 0:
                out_final.append(P.emit("sync",
                                        (lambda tt=tt, x2a=x2a: (lambda e: e.dma_start(
                                            out=dbg_h["x2"][tt * 128:(tt + 1) * 128, :], in_=x2a)))(),
                                        waits=[x2tok], inc="dO", amt=16))
            ha = hbuf[:, sl * E:(sl + 1) * E]
            P.emit("vector", (lambda x2a=x2a: (lambda v: v.bn_stats(stats[:, 0:6], x2a[:, 0:512])))(), waits=[x2tok])
            P.emit("vector", (lambda x2a=x2a: (lambda v: v.bn_stats(stats[:, 6:12], x2a[:, 512:1024])))())
            P.emit("vector", lambda v: v.drain())
            P.emit("vector", lambda v: v.bn_aggr(stats[:, 12:14], stats[:, 0:12]))
            P.emit("vector", lambda v: v.drain())
            P.emit("vector", lambda v: v.tensor_scalar_mul(stats[:, 14:15], stats[:, 12:13], -1.0))
            vt = P.emit("vector", lambda v: v.tensor_copy(stats[:, 15:16], stats[:, 13:14]), inc="tV")
            st = P.emit("scalar", lambda s: s.activation(stats[:, 16:17], stats[:, 15:16], ACT.Sqrt,
                                                         bias=stats[:, 30:31]),
                        waits=[vt, eps_tok], inc="tS")
            P.emit("vector", lambda v: v.reciprocal(stats[:, 17:18], stats[:, 16:17]), waits=[st])
            P.emit("vector", lambda v: v.drain())
            h2tok = P.emitd("vector",
                           (lambda x2a=x2a, ha=ha: (lambda v: v.tensor_scalar(
                               ha, x2a, stats[:, 14:15], stats[:, 17:18], ALU.add, ALU.mult)))(),
                           inc="tV")
            for half in range(2):
                bk = 2 + half
                w0 = [h2tok, tokb[bk]]
                mt = None
                for j in range(4):
                    e = half * 4 + j
                    mt = mm(ha[:, e * 128:(e + 1) * 128], iden, bank(bk, j * 128, (j + 1) * 128),
                            start=True, stop=True, waits=w0 if j == 0 else ())
                tokb[bk] = P.emitd("vector",
                                  (lambda bk=bk, half=half, h2Tp=h2Tp: (lambda v: v.tensor_copy(
                                      h2Tp[:, half * 512:(half + 1) * 512], bank(bk))))(),
                                  waits=[mt], inc="tV")
            h2T_toks[tt] = tokb[3]

        def cd_stage2b(tt):
            sl = tt % 2
            h2Tp = h2T[:] if sl == 0 else G[:, 0:1024]
            for half in range(2):
                bk = 4 + half
                w0 = [tokb[bk], h2T_toks[tt], tokb[2]]
                for e in range(8):
                    mm(h2Tp[:, e * 128:(e + 1) * 128],
                       C[:, e * E + half * 512: e * E + (half + 1) * 512], bank(bk),
                       start=(e == 0), stop=(e == 7), waits=w0 if e == 0 else ())
            xh_toks[tt] = P.tick("tT")

        def cd_stage3(tt):
            sl = tt % 2
            xh_tok = xh_toks[tt]
            P.emit("vector", lambda v: v.reduce_max(stats[:, 20:21], PS[:, 4 * 512:6 * 512], AX.X,
                                                    apply_absolute_value=True), waits=[xh_tok])
            P.emit("vector", lambda v: v.drain())
            P.emit("vector", lambda v: v.tensor_scalar_max(stats[:, 21:22], stats[:, 20:21], 1e-5))
            P.emit("vector", lambda v: v.drain())
            P.emit("vector", lambda v: v.reciprocal(stats[:, 22:23], stats[:, 21:22]))
            P.emit("vector", lambda v: v.drain())
            P.emit("vector", lambda v: v.tensor_scalar_mul(stats[:, 23:24], stats[:, 22:23], 7.0))
            P.emit("vector", lambda v: v.tensor_scalar_mul(stats[:, 24:25], stats[:, 21:22], 1.0 / 7.0))
            P.emit("vector", lambda v: v.drain())
            Tbuf = Fq[:, 0:E]
            tp_ = P.emit("vector",
                         (lambda Tbuf=Tbuf: (lambda v: v.tensor_scalar(
                             Tbuf, PS[:, 4 * 512:6 * 512], stats[:, 23:24], 8.5, ALU.mult, ALU.add)))(),
                         inc="tV")
            tokb[4] = tp_
            tokb[5] = tp_
            tp_toks[tt] = tp_

        def cd_stage3b(tt):
            sl = tt % 2
            qibufp = qibuf[:] if sl == 0 else KBs[:, 0:1024]
            qiTp = qiT[:] if sl == 0 else G[:, 1024:2048]
            Tbuf = Fq[:, 0:E]
            r_cur = Fq[:, E:2 * E]
            bbuf = bbufq[:]
            off = 0.0
            first = True
            prev_v = tp_toks[tt]
            for c in (8.0, 4.0, 2.0, 1.0):
                srcb = Tbuf if first else r_cur
                sg = P.emit("scalar", (lambda srcb=srcb: (lambda s: s.activation(
                    bbuf, srcb, ACT.Sign, bias=stats[:, 29:30])))(),
                    waits=[prev_v, m8_tok], inc="tS")
                prev_v = P.emit("vector", (lambda srcb=srcb, r_cur=r_cur, c=c: (lambda v: v.scalar_tensor_tensor(
                    r_cur, bbuf, -c / 2.0, srcb, ALU.mult, ALU.add)))(),
                    waits=[sg], inc="tV")
                off += c / 2.0
                first = False
            qitok = P.emitd("vector",
                           (lambda Tbuf=Tbuf, r_cur=r_cur, qibufp=qibufp, off=off: (lambda v: v.scalar_tensor_tensor(
                               qibufp, Tbuf, off - 8.0, r_cur, ALU.add, ALU.subtract)))(),
                           waits=[prev_v], inc="tV")
            for half in range(2):
                bk = 2 + half
                w0 = [qitok, tokb[bk]]
                mt = None
                for j in range(4):
                    e = half * 4 + j
                    mt = mm(qibufp[:, e * 128:(e + 1) * 128], iden, bank(bk, j * 128, (j + 1) * 128),
                            start=True, stop=True, waits=w0 if j == 0 else ())
                tokb[bk] = P.emitd("vector",
                                  (lambda bk=bk, half=half, qiTp=qiTp: (lambda v: v.tensor_copy(
                                      qiTp[:, half * 512:(half + 1) * 512], bank(bk))))(),
                                  waits=[mt], inc="tV")
            qiT_tok = tokb[3]
            ffn_mt = {}
            for half in range(2):
                bk = 6 + half
                w0 = [tokb[bk], qiT_tok]
                mt = None
                for ic in range(8):
                    mt = mm(qiTp[:, ic * 128:(ic + 1) * 128],
                            C[:, 8 * E + ic * E + half * 512: 8 * E + ic * E + (half + 1) * 512],
                            bank(bk), start=(ic == 0), stop=(ic == 7), waits=w0 if ic == 0 else ())
                ffn_mt[half] = mt
            ffn_mts[tt] = ffn_mt

        def cd_finish(tt):
            sl = tt % 2
            ffn_mt = ffn_mts[tt]
            for half in range(2):
                bk = 6 + half
                tokb[bk] = P.emitd("vector",
                                  (lambda bk=bk, sl=sl, half=half: (lambda v: v.scalar_tensor_tensor(
                                      obuf[:, half * 512:(half + 1) * 512],
                                      bank(bk), stats[:, 28:29] if False else stats[:, 24 + (tt % 2) * 0:25],
                                      x2buf[:, sl * E + half * 512: sl * E + (half + 1) * 512],
                                      ALU.mult, ALU.add)))(),
                                  waits=[ffn_mt[half], ostore.get(tt - 1)], inc="tV")
            ostore[tt] = P.emit("sync",
                                (lambda tt=tt, ob=ob: (lambda e: e.dma_start(
                                    out=ob[tt * 128:(tt + 1) * 128, :],
                                    in_=obuf[:])))(),
                                waits=[tokb[7]], inc="dO", amt=16)
            out_final.append(ostore[tt])
            x2free[tt] = tokb[7]

        cd_stage1(0)
        cd_stage2(0)
        cd_stage2b(0)
        for tt in range(NT):
            if tt + 1 < NT:
                cd_stage1(tt + 1)
            cd_stage3(tt)
            cd_stage3b(tt)
            if tt + 1 < NT:
                cd_stage2(tt + 1)
            cd_finish(tt)
            if tt + 1 < NT:
                cd_stage2b(tt + 1)

    P.emit("sync", lambda e: e.nop(), waits=[P.tick("dO")])

    with nc.Block() as block:
        es = P.build(nc, block)
    nc._keepalive = es
    return nc


_BUILD_CACHE = {}


def get_nc(dbg=False):
    if dbg not in _BUILD_CACHE:
        _BUILD_CACHE[dbg] = build_nc(dbg)
    return _BUILD_CACHE[dbg]


def make_inmaps(inputs, dbg=False):
    x = np.ascontiguousarray(np.asarray(inputs["L0_attn_x"], np.float32))
    consts = host_consts()
    bq = np.asarray(inputs["L0_attn_bq"], np.float32)
    bkv = np.asarray(inputs["L0_attn_bkv"], np.float32)
    bo = np.asarray(inputs["L0_attn_bo"], np.float32)
    common = {
        "wq_w": np.ascontiguousarray(np.asarray(inputs["L0_attn_Wq"], np.float32)),
        "wkv_w": np.ascontiguousarray(np.asarray(inputs["L0_attn_Wkv"], np.float32)),
        "wo_w": np.ascontiguousarray(np.asarray(inputs["L0_attn_Wo"], np.float32)),
        "wf_w": np.ascontiguousarray(np.asarray(inputs["L0_ffn_weight"], np.float32)),
        "bq_t": np.ascontiguousarray(bq.reshape(8, 128).T),
        "bk_t": np.ascontiguousarray(bkv[:E].reshape(8, 128).T),
        "bv_row": np.ascontiguousarray(bkv[E:].reshape(1, E)),
        "bo_row": np.ascontiguousarray(bo.reshape(1, E)),
        "cos_t": consts["cos_t"], "sin_t": consts["sin_t"],
        "had": consts["had"], "iden": consts["iden"], "perm": consts["perm"],
        "perm2": consts["perm2"], "sel": consts["sel"],
        "ones512": consts["ones512"], "onescol": consts["onescol"],
        "epsrow": consts["epsrow"], "ltri": consts["ltri"],
    }
    in_maps = []
    for c in range(8):
        m = dict(common)
        m["x_in"] = np.ascontiguousarray(x[2 * c:2 * c + 2])
        in_maps.append(m)
    return in_maps


def kernel(**inputs):
    nc = get_nc(dbg=False)
    in_maps = make_inmaps(inputs)
    res = run_bass_kernel_spmd(nc, in_maps, core_ids=list(range(8)))
    out = np.concatenate([np.asarray(r["out"]) for r in res.results], axis=0)
    return out.astype(np.float32)


# version 15
# speedup vs baseline: 1.0304x; 1.0304x over previous
import numpy as np
import ml_dtypes
import concourse.bass as bass
import concourse.mybir as mybir
from concourse.bass_utils import run_bass_kernel_spmd

F32 = mybir.dt.float32
BF16 = mybir.dt.bfloat16
BF = ml_dtypes.bfloat16
ALU = mybir.AluOpType
ACT = mybir.ActivationFunctionType
AX = mybir.AxisListType

S = 2048
E = 1024
NT = S // 128
NP = 8
NB = 2
NQT = S // 512

ENGINES = ("sync", "scalar", "vector", "tensor", "gpsimd")


class Prog:

    def __init__(self):
        self.ops = {e: [] for e in ENGINES}
        self.cnt = {}

    def emit(self, eng, fn, waits=(), inc=None, amt=1):
        waits = [w for w in waits if w is not None]
        self.ops[eng].append((tuple(waits), fn, inc, amt))
        if inc is not None:
            self.cnt[inc] = self.cnt.get(inc, 0) + amt
            return (inc, self.cnt[inc])
        return None

    def tick(self, sem):
        return (sem, self.cnt.get(sem, 0))

    def barrier(self):
        ticks = [self.tick(s) for s in list(self.cnt)]
        for e in ENGINES:
            if self.ops[e]:
                self.emit(e, lambda eng: eng.nop(), waits=ticks)

    def emitd(self, eng, fn, waits=(), inc=None, amt=1):
        return self.emit(eng, fn, waits=waits, inc=inc, amt=amt)

    def build(self, nc, block):
        from contextlib import ExitStack
        es = ExitStack()
        names = set(self.cnt)
        for ops in self.ops.values():
            for w, _, _, _ in ops:
                for sname, _v in w:
                    names.add(sname)
        sems = {n: es.enter_context(nc.semaphore(n)) for n in sorted(names)}

        def mk_body(engname):
            def body(eng):
                seen = {}
                for waits, fn, inc, amt in self.ops[engname]:
                    for sname, val in waits:
                        if val > 0 and seen.get(sname, 0) < val:
                            eng.wait_ge(sems[sname], val)
                            seen[sname] = val
                    ins = fn(eng)
                    if inc is not None:
                        assert ins is not None, f"no instruction returned on {engname}"
                        ins.then_inc(sems[inc], amt)
            return body

        for ename in ENGINES:
            if self.ops[ename]:
                getattr(block, ename)(mk_body(ename))
        return es


def rope_tables_np():
    D = 64
    inv_freq = (1.0 / (np.float32(10000.0) ** (np.arange(0, D, 2, dtype=np.float32) / np.float32(D)))).astype(np.float32)
    pos = np.arange(S, dtype=np.float32)
    ang = (pos[:, None] * inv_freq[None, :]).astype(np.float32)
    cos = np.cos(ang).astype(np.float32)
    sin = np.sin(ang).astype(np.float32)
    cos_t = np.empty((128, S), np.float32)
    sin_t = np.empty((128, S), np.float32)
    for d in range(128):
        j = d % 64
        if j < 32:
            cos_t[d] = cos[:, j]
            sin_t[d] = -sin[:, j]
        else:
            cos_t[d] = cos[:, j - 32]
            sin_t[d] = sin[:, j - 32]
    return cos_t, sin_t


def hadamard_np():
    h = np.array([[1.0]], dtype=np.float64)
    while h.shape[0] < E:
        h = np.block([[h, h], [h, -h]])
    return (h / np.sqrt(E)).astype(np.float32)


def host_consts():
    cos_t, sin_t = rope_tables_np()
    had = hadamard_np().astype(BF)
    iden = np.eye(128, dtype=np.float32).astype(BF)
    perm = np.zeros((128, 128), np.float32)
    for m in range(128):
        base = (m // 64) * 64
        j = m % 64
        perm[base + (j + 32) % 64, m] = 1.0
    perm = perm.astype(BF)
    perm2 = np.zeros((128, 128), np.float32)
    for m in range(128):
        perm2[(m + 64) % 128, m] = 1.0
    perm2 = perm2.astype(BF)
    sel = np.zeros((128, 128), np.float32)
    sel[0, :64] = 1.0
    sel[32, 64:] = 1.0
    sel = sel.astype(BF)
    ltri = (np.arange(128)[None, :] >= np.arange(128)[:, None]).astype(np.float32).astype(BF)
    ones512 = np.ones((1, 512), np.float32).astype(BF)
    onescol = np.ones((128, 1), np.float32).astype(BF)
    epsrow = np.full((1, 128), 1e-30, np.float32).astype(BF)
    return dict(cos_t=cos_t, sin_t=sin_t, had=had, iden=iden, perm=perm,
                perm2=perm2, sel=sel, ltri=ltri, ones512=ones512, onescol=onescol,
                epsrow=epsrow)


def build_nc(dbg=False):
    nc = bass.Bass("TRN2", target_bir_lowering=False)
    P = Prog()

    x_in = nc.dram_tensor("x_in", [NB, S, E], F32, kind="ExternalInput").ap()
    wq_w = nc.dram_tensor("wq_w", [E, E], F32, kind="ExternalInput").ap()
    wkv_w = nc.dram_tensor("wkv_w", [E, 2 * E], F32, kind="ExternalInput").ap()
    wo_w = nc.dram_tensor("wo_w", [E, E], F32, kind="ExternalInput").ap()
    wf_w = nc.dram_tensor("wf_w", [E, E], F32, kind="ExternalInput").ap()
    bq_t = nc.dram_tensor("bq_t", [128, 8], F32, kind="ExternalInput").ap()
    bk_t = nc.dram_tensor("bk_t", [128, 8], F32, kind="ExternalInput").ap()
    bv_row = nc.dram_tensor("bv_row", [1, E], F32, kind="ExternalInput").ap()
    bo_row = nc.dram_tensor("bo_row", [1, E], F32, kind="ExternalInput").ap()
    cos_h = nc.dram_tensor("cos_t", [128, S], F32, kind="ExternalInput").ap()
    sin_h = nc.dram_tensor("sin_t", [128, S], F32, kind="ExternalInput").ap()
    had_h = nc.dram_tensor("had", [E, E], BF16, kind="ExternalInput").ap()
    iden_h = nc.dram_tensor("iden", [128, 128], BF16, kind="ExternalInput").ap()
    perm_h = nc.dram_tensor("perm", [128, 128], BF16, kind="ExternalInput").ap()
    perm2_h = nc.dram_tensor("perm2", [128, 128], BF16, kind="ExternalInput").ap()
    sel_h = nc.dram_tensor("sel", [128, 128], BF16, kind="ExternalInput").ap()
    ltri_h = nc.dram_tensor("ltri", [128, 128], BF16, kind="ExternalInput").ap()
    ones_h = nc.dram_tensor("ones512", [1, 512], BF16, kind="ExternalInput").ap()
    onescol_h = nc.dram_tensor("onescol", [128, 1], BF16, kind="ExternalInput").ap()
    eps_h = nc.dram_tensor("epsrow", [1, 128], BF16, kind="ExternalInput").ap()
    out_h = nc.dram_tensor("out", [NB, S, E], F32, kind="ExternalOutput").ap()
    wqT_d = nc.dram_tensor("wqT_d", [128, 8 * E], BF16, kind="Internal").ap()
    dbg_h = {}
    if dbg:
        for name, shape, dt in [("hT", [128, NP * S], BF16), ("QT", [128, NP * S], BF16),
                                ("KT", [128, NP * S], BF16), ("Vt", [128, NT * E], BF16),
                                ("OT", [128, NP * S], BF16), ("x2", [S, E], F32),
                                ("wqT", [128, 8 * E], BF16)]:
            dbg_h[name] = nc.dram_tensor("dbg_" + name, shape, dt, kind="ExternalOutput").ap()

    A = nc.alloc_sbuf_tensor("A_qt", [128, NP * S], BF16)
    B = nc.alloc_sbuf_tensor("B_kt", [128, NP * S], BF16)
    C = nc.alloc_sbuf_tensor("C_v", [128, NT * E], BF16)
    D = nc.alloc_sbuf_tensor("D_ht", [128, NP * S], BF16)
    Ew = nc.alloc_sbuf_tensor("E_w", [128, 8 * E], BF16)
    tabbuf = nc.alloc_sbuf_tensor("tabbuf", [128, 1024], F32)
    Fq = nc.alloc_sbuf_tensor("F_q", [128, 2 * E], F32)
    bbufq = nc.alloc_sbuf_tensor("bbufq", [128, E], BF16)
    G = nc.alloc_sbuf_tensor("G_pt", [128, 2048], BF16)
    KBs = nc.alloc_sbuf_tensor("KB_s", [128, S], BF16)
    QBs = nc.alloc_sbuf_tensor("QB_s", [128, 2 * 512], BF16)
    QAs = nc.alloc_sbuf_tensor("QA_s", [128, 2 * 512], BF16)
    Wstr = nc.alloc_sbuf_tensor("W_str", [128, 2 * E], BF16)
    xbuf = nc.alloc_sbuf_tensor("xbuf", [128, 2 * E], F32)
    x2buf = nc.alloc_sbuf_tensor("x2buf", [128, 2 * E], F32)
    hbuf = nc.alloc_sbuf_tensor("hbuf", [128, 2 * E], BF16)
    qibuf = nc.alloc_sbuf_tensor("qibuf", [128, E], BF16)
    qiT = nc.alloc_sbuf_tensor("qiT", [128, E], BF16)
    h2T = qiT
    wqtile = qibuf
    recip_sb = nc.alloc_sbuf_tensor("recip_sb", [128, 512], BF16)
    stats = nc.alloc_sbuf_tensor("stats", [128, 32], F32)
    consts = nc.alloc_sbuf_tensor("consts", [128, 5 * 128], BF16)
    onesc = nc.alloc_sbuf_tensor("onesc", [128, 1], BF16)
    bqs = nc.alloc_sbuf_tensor("bqs", [128, 8], F32)
    bks = nc.alloc_sbuf_tensor("bks", [128, 8], F32)
    bvr_t = nc.alloc_sbuf_tensor("bvr_t", [1, E], BF16)
    bor_t = nc.alloc_sbuf_tensor("bor_t", [1, E], BF16)
    onesr_t = nc.alloc_sbuf_tensor("onesr_t", [1, 512], BF16)
    epsr_t = nc.alloc_sbuf_tensor("epsr_t", [1, 128], BF16)
    bvr = bvr_t[0:1, 0:E]
    bor = bor_t[0:1, 0:E]
    onesr = onesr_t[0:1, 0:512]
    epsr = epsr_t[0:1, 0:128]
    zror = epsr

    iden = consts[:, 0:128]
    perm = consts[:, 128:256]
    perm2 = consts[:, 256:384]
    sel = consts[:, 384:512]
    ltri = consts[:, 512:640]
    COSs = tabbuf[:, 0:512]
    SINs = tabbuf[:, 512:1024]
    obuf = tabbuf[:, 0:E]
    ropeb = Fq[:, 0:1024]
    lnscr = Fq[:, 0:512]
    Rcopy = Fq[:, 512:1024]
    wf32 = Fq[:, 0:E]

    PS = nc.alloc_psum_tensor("PS", [128, 4096], F32)

    def bank(i, lo=0, hi=512, p0=0, p1=128):
        return PS[p0:p1, i * 512 + lo:i * 512 + hi]

    tokb = [None] * 8

    def mm(lhsT, rhs, out, start, stop, waits=(), tp=None):
        def f(t, lhsT=lhsT, rhs=rhs, out=out, start=start, stop=stop, tp=tp):
            return t.matmul(out, lhsT, rhs, start=start, stop=stop, tile_position=tp)
        P.emit("tensor", f, waits=waits, inc="tT")
        return ("tT", P.cnt["tT"])

    cl = []
    for dst, src in [(consts[:, 0:128], iden_h), (consts[:, 128:256], perm_h),
                     (consts[:, 256:384], perm2_h), (consts[:, 384:512], sel_h),
                     (consts[:, 512:640], ltri_h),
                     (onesr, ones_h), (onesc[:], onescol_h), (epsr, eps_h),
                     (bqs[:], bq_t), (bks[:], bk_t),
                     ]:
        cl.append(P.emit("sync", (lambda d, s: (lambda e: e.dma_start(out=d, in_=s)))(dst, src),
                         inc="dS", amt=16))
    cl.append(P.emit("gpsimd", lambda g: g.dma_start(out=bvr, in_=bv_row), inc="dG", amt=16))
    cl.append(P.emit("gpsimd", lambda g: g.dma_start(out=bor, in_=bo_row), inc="dG", amt=16))
    eps_tok = P.emit("vector", lambda v: v.memset(stats[:, 30:31], 1e-5), inc="tV")
    m8_tok = P.emit("vector", lambda v: v.memset(stats[:, 29:30], -8.0), inc="tV")
    cl.append(m8_tok)
    P.emit("vector", lambda v: v.memset(QBs[:], 0.0))
    P.emit("vector", lambda v: v.memset(QAs[:], 0.0))
    zpad_tok = P.emit("vector", lambda v: v.memset(KBs[:], 0.0), inc="tV")
    cl.append(zpad_tok)
    cl.append(eps_tok)
    consts_tok = cl[:]

    out_final = []

    for b in range(NB):
        xb = x_in[b]
        ob = out_h[b]
        prev_done = [P.tick("tT"), P.tick("tV"), P.tick("tS")]

        P.barrier()
        wstr_last = {}
        rope_cons = {}
        wv_toks = []
        for e in range(8):
            wv_toks.append(P.emit(
                "gpsimd",
                (lambda e=e: (lambda g: g.dma_start(
                    out=Ew[:, e * E:(e + 1) * E],
                    in_=wkv_w[e * 128:(e + 1) * 128, E:2 * E])))(),
                waits=prev_done, inc="dG", amt=16))

        httok = {}
        hnorm = {}
        xcons = {}
        hmm_t = {}

        for tt in range(NT):
            sl = tt % 2
            xa = xbuf[:, sl * E:(sl + 1) * E]
            ha = hbuf[:, sl * E:(sl + 1) * E]
            xl = P.emit("sync",
                        (lambda tt=tt, xa=xa, xb=xb: (lambda e: e.dma_start(
                            out=xa, in_=xb[tt * 128:(tt + 1) * 128, :])))(),
                        waits=[xcons.get(tt - 2)] + (prev_done if tt < 2 else []),
                        inc="dS", amt=16)
            P.emit("vector", (lambda xa=xa: (lambda v: v.bn_stats(stats[:, 0:6], xa[:, 0:512])))(), waits=[xl])
            P.emit("vector", (lambda xa=xa: (lambda v: v.bn_stats(stats[:, 6:12], xa[:, 512:1024])))())
            P.emit("vector", lambda v: v.drain())
            P.emit("vector", lambda v: v.bn_aggr(stats[:, 12:14], stats[:, 0:12]))
            P.emit("vector", lambda v: v.drain())
            P.emit("vector", lambda v: v.tensor_scalar_mul(stats[:, 14:15], stats[:, 12:13], -1.0))
            vt = P.emit("vector", lambda v: v.tensor_copy(stats[:, 15:16], stats[:, 13:14]), inc="tV")
            st = P.emit("scalar", lambda s: s.activation(stats[:, 16:17], stats[:, 15:16], ACT.Sqrt,
                                                         bias=stats[:, 30:31]),
                        waits=[vt, eps_tok], inc="tS")
            P.emit("vector", lambda v: v.reciprocal(stats[:, 17:18], stats[:, 16:17]), waits=[st])
            P.emit("vector", lambda v: v.drain())
            hnorm[tt] = P.emitd("vector",
                               (lambda xa=xa, ha=ha: (lambda v: v.tensor_scalar(
                                   ha, xa, stats[:, 14:15], stats[:, 17:18], ALU.add, ALU.mult)))(),
                               waits=[hmm_t.get(tt - 2)], inc="tV")
            xcons[tt] = hnorm[tt]
            httok[tt] = []
            for half in range(2):
                bk = half
                w0 = [hnorm[tt], tokb[bk], consts_tok[0]] + (prev_done if tt < 2 else [])
                mt = None
                for j in range(4):
                    e = half * 4 + j
                    mt = mm(ha[:, e * 128:(e + 1) * 128], iden,
                            bank(bk, j * 128, (j + 1) * 128),
                            start=True, stop=True, waits=w0 if j == 0 else ())
                dst = D[:].rearrange("p (c t) -> p c t", c=NP)[:, half * 4:(half + 1) * 4,
                                                              tt * 128:(tt + 1) * 128]
                src = bank(bk).rearrange("p (c t) -> p c t", c=4)
                tokb[bk] = P.emitd("vector",
                                  (lambda dst=dst, src=src: (lambda v: v.tensor_copy(dst, src)))(),
                                  waits=[mt], inc="tV")
                httok[tt].append(tokb[bk])
            hmm_t[tt] = P.tick("tT")
            for half in range(2):
                bk = 2 + half
                w0 = [tokb[bk], wv_toks[-1], consts_tok[4]] + httok[tt]
                mm(onesr[0:1, 0:128], bvr[0:1, half * 512:(half + 1) * 512], bank(bk),
                   start=True, stop=False, waits=w0)
                mt = None
                for e in range(8):
                    mt = mm(D[:, e * S + tt * 128: e * S + (tt + 1) * 128],
                            Ew[:, e * E + half * 512: e * E + (half + 1) * 512], bank(bk),
                            start=False, stop=(e == 7))
                tokb[bk] = P.emitd("vector",
                                  (lambda bk=bk, tt=tt, half=half: (lambda v: v.tensor_copy(
                                      C[:, tt * E + half * 512: tt * E + (half + 1) * 512], bank(bk))))(),
                                  waits=[mt], inc="tV")

            if tt % 4 == 3:
                tg = tt // 4
                tabtok = P.emit("sync",
                                (lambda tg=tg: (lambda e: e.dma_start(
                                    out=COSs, in_=cos_h[:, tg * 512:(tg + 1) * 512])))(),
                                waits=[rope_cons.get(tg - 1)], inc="dS", amt=16)
                tabtok2 = P.emit("sync",
                                 (lambda tg=tg: (lambda e: e.dma_start(
                                     out=SINs, in_=sin_h[:, tg * 512:(tg + 1) * 512])))(),
                                 inc="dS", amt=16)
                def emit_rope(dslice, ev, tg=tg, tabtok=tabtok, tabtok2=tabtok2):
                    mt2 = mm(perm, dslice, bank(6), start=True, stop=True,
                             waits=[tokb[6], ev])
                    p1 = P.emit("vector",
                                lambda v: v.tensor_mul(ropeb[:, 0:512], bank(6), SINs),
                                waits=[mt2, tabtok, tabtok2], inc="tV")
                    tokb[6] = p1
                    P.emit("vector",
                           (lambda dslice=dslice: (lambda v: v.tensor_mul(
                               ropeb[:, 512:1024], dslice, COSs)))(),
                           inc="tV")
                    rope_cons[tg] = P.emitd("vector",
                                            (lambda dslice=dslice: (lambda v: v.tensor_add(
                                                dslice, ropeb[:, 0:512], ropeb[:, 512:1024])))(),
                                            inc="tV")

                rope_pend = None
                for ti, (Wsrc, dstT, bias_t) in enumerate([(wq_w, A, bqs), (wkv_w, B, bks)]):
                    for p in range(NP):
                        ssl = p % 2
                        wtok = P.emit(
                            "gpsimd",
                            (lambda Wsrc=Wsrc, p=p, ssl=ssl: (lambda g: g.dma_start(
                                out=Wstr[:, ssl * E:(ssl + 1) * E].rearrange("q (c t) -> q c t", c=8),
                                in_=Wsrc.rearrange("(c q) d -> q c d", q=128)[:, :, p * 128:(p + 1) * 128])))(),
                            waits=[wstr_last.get(ssl)], inc="dG", amt=16)
                        bk = 4 + (p % 2)
                        w0 = [tokb[bk], wtok] + httok[tt]
                        mt = None
                        for e in range(8):
                            mt = mm(Wstr[:, ssl * E + e * 128: ssl * E + (e + 1) * 128],
                                    D[:, e * S + tg * 512: e * S + (tg + 1) * 512], bank(bk),
                                    start=(e == 0), stop=(e == 7), waits=w0 if e == 0 else ())
                        wstr_last[ssl] = P.tick("tT")
                        dslice = dstT[:, p * S + tg * 512: p * S + (tg + 1) * 512]
                        ev = P.emitd(
                            "scalar",
                            (lambda bk=bk, dslice=dslice, bias_t=bias_t, p=p: (lambda s: s.activation(
                                dslice, bank(bk), ACT.Identity, bias=bias_t[:, p:p + 1])))(),
                            waits=[mt], inc="tS")
                        tokb[bk] = ev
                        if rope_pend is not None:
                            emit_rope(*rope_pend)
                        rope_pend = (dslice, ev)
                if rope_pend is not None:
                    emit_rope(*rope_pend)
                    rope_pend = None

        rope_done = [P.tick("tV")]
        if dbg and b == 0:
            for name, src in [("hT", D[:]), ("QT", A[:]), ("KT", B[:]), ("Vt", C[:])]:
                out_final.append(P.emit("sync",
                                        (lambda name=name, src=src: (lambda e: e.dma_start(
                                            out=dbg_h[name], in_=src)))(),
                                        waits=rope_done + [P.tick("tT"), P.tick("tS")],
                                        inc="dO", amt=16))

        P.barrier()
        ot_last = None
        pending = None

        def emit_closing(hp, qt, sbk):
            nonlocal ot_last
            smt = mm(epsr[0:1, :], onesr[0:1, :], bank(sbk), start=False, stop=True)
            l1 = P.emit("scalar", (lambda sbk=sbk: (lambda s: s.activation(
                lnscr[0:33, :], bank(sbk, p0=0, p1=33), ACT.Ln)))(),
                waits=[smt], inc="tS")
            l2 = P.emit("scalar", lambda s: s.activation(recip_sb[0:33, :], lnscr[0:33, :],
                                                         ACT.Exp, scale=-1.0),
                        waits=[l1], inc="tS")
            tokb[sbk] = l2
            rt = mm(sel[0:33, :], recip_sb[0:33, :], bank(4), start=True, stop=True,
                    waits=[tokb[4], l2])
            rc = P.emit("vector", lambda v: v.tensor_copy(Rcopy, bank(4)),
                        waits=[rt], inc="tV")
            tokb[4] = rc
            dsl = D[:, hp * S + qt * 512: hp * S + (qt + 1) * 512]
            ot = P.emit("vector",
                        (lambda dsl=dsl: (lambda v: v.tensor_mul(dsl, bank(2), Rcopy)))(),
                        waits=[rc], inc="tV")
            tokb[2] = ot
            ot_last = ot

        for hp in range(NP):
            kb_toks = []
            kb_banks = (0, 1, 6, 7)
            for j in range(4):
                bkj = kb_banks[j]
                mm(perm2, B[:, hp * S + j * 512: hp * S + (j + 1) * 512], bank(bkj),
                   start=True, stop=True, waits=[tokb[bkj]] + rope_done)
            kmt = P.tick("tT")
            for half in range(2):
                pbase = 0 if half == 0 else 3072
                tk = P.emit("vector",
                            (lambda half=half, pbase=pbase: (lambda v: v.tensor_copy(
                                KBs[0:64, half * 1024:(half + 1) * 1024],
                                PS[0:64, pbase:pbase + 1024])))(),
                            waits=[kmt], inc="tV")
                tokb[kb_banks[2 * half]] = tk
                tokb[kb_banks[2 * half + 1]] = tk
                kb_toks.append(tk)
            def stage_q(qt, hp=hp):
                qp = qt % 2
                mt = mm(perm2, A[:, hp * S + qt * 512: hp * S + (qt + 1) * 512], bank(4),
                        start=True, stop=True, waits=[tokb[4], q_cons.get(qp)] + rope_done)
                qb_tok = P.emit("vector",
                                (lambda qp=qp: (lambda v: v.tensor_copy(
                                    QBs[0:64, qp * 512:(qp + 1) * 512], bank(4, p0=0, p1=64))))(),
                                waits=[mt], inc="tV")
                tokb[4] = qb_tok
                qa_tok = P.emit("vector",
                                (lambda hp=hp, qt=qt, qp=qp: (lambda v: v.tensor_copy(
                                    QAs[0:64, qp * 512:(qp + 1) * 512],
                                    A[0:64, hp * S + qt * 512: hp * S + (qt + 1) * 512])))(),
                                waits=rope_done + [mt, q_cons.get(qp)], inc="tV")
                q_toks[qp] = (qa_tok, qb_tok)

            q_cons = {}
            q_toks = {}
            stage_q(0)
            for qt in range(NQT):
                nk = 4 * (qt + 1)
                qp = qt % 2
                sbk = 3 if qt % 2 == 0 else 5
                qa_tok, qb_tok = q_toks[qp]
                pt_tok = [None, None]
                s_tok = {}

                def emit_S(kt, hp=hp, qt=qt, qp=qp, kb_last=kb_toks[-1], qb_tok=qb_tok,
                           qa_tok=qa_tok, s_tok=s_tok):
                    bA = 0 if kt % 2 == 0 else 6
                    bB = bA + 1
                    mm(B[:, hp * S + kt * 128: hp * S + (kt + 1) * 128],
                       QAs[:, qp * 512:(qp + 1) * 512], bank(bA),
                       start=True, stop=True, waits=[tokb[bA], qa_tok, zpad_tok] + rope_done)
                    mm(KBs[:, kt * 128:(kt + 1) * 128], QBs[:, qp * 512:(qp + 1) * 512], bank(bB),
                       start=True, stop=True, waits=[tokb[bB], kb_last, qb_tok])
                    s_tok[kt] = P.tick("tT")

                emit_S(0)
                if nk > 1:
                    emit_S(1)
                mm(epsr[0:1, :], onesr[0:1, :], bank(sbk), start=True, stop=False,
                   waits=[tokb[sbk]])
                if pending is not None:
                    emit_closing(*pending)
                    pending = None
                for kt in range(nk):
                    par = kt % 2
                    bA = 0 if par == 0 else 6
                    pt_a = G[:, par * 1024: par * 1024 + 512]
                    pt_b = G[:, par * 1024 + 512: par * 1024 + 1024]
                    joff = max(0, kt * 128 - qt * 512)
                    if joff:
                        def expfn(s, par=par, bA=bA, joff=joff):
                            gi = G[:, par * 1024:(par + 1) * 1024].rearrange(
                                "p (h q) -> p h q", h=2)[:, :, joff:512]
                            pi = PS[:, bA * 512:(bA + 2) * 512].rearrange(
                                "p (h q) -> p h q", h=2)[:, :, joff:512]
                            return s.activation(gi, pi, ACT.Exp, scale=0.125)
                    else:
                        def expfn(s, par=par, bA=bA):
                            return s.activation(G[:, par * 1024:(par + 1) * 1024],
                                                PS[:, bA * 512:(bA + 2) * 512],
                                                ACT.Exp, scale=0.125)
                    etok = P.emitd("scalar", expfn,
                                  waits=[s_tok[kt], pt_tok[par]], inc="tS")
                    tokb[bA] = etok
                    tokb[bA + 1] = etok
                    ptok = etok
                    if kt >= 4 * qt:
                        joff = kt * 128 - qt * 512
                        slot = G[:, par * 1024:(par + 1) * 1024].rearrange("p (h q) -> p h q", h=2)
                        if joff:
                            P.emit("vector",
                                   (lambda slot=slot, joff=joff: (lambda v: v.memset(
                                       slot[:, :, 0:joff], 0.0)))(),
                                   waits=[etok], inc="tV")
                        ptok = P.emit("vector",
                                      (lambda slot=slot, joff=joff: (lambda v: v.tensor_mul(
                                          slot[:, :, joff:joff + 128], slot[:, :, joff:joff + 128],
                                          ltri.rearrange("p (a q) -> p a q", a=1).broadcast_to([128, 2, 128]))))(),
                                      waits=[etok], inc="tV")
                    if kt + 2 < nk:
                        emit_S(kt + 2)
                    pvw = [ptok] + ([tokb[2]] if kt == 0 else [])
                    mm(C[:, kt * E + (2 * hp) * 64: kt * E + (2 * hp) * 64 + 64], pt_a,
                       bank(2, p0=0, p1=64), start=(kt == 0), stop=(kt == nk - 1),
                       waits=pvw, tp=(0, 0))
                    mm(C[:, kt * E + (2 * hp + 1) * 64: kt * E + (2 * hp + 1) * 64 + 64], pt_b,
                       bank(2, p0=64, p1=128), start=(kt == 0), stop=(kt == nk - 1),
                       tp=(0, 64))
                    mm(onesc[:], pt_a, bank(sbk, p0=0, p1=1), start=False, stop=False, tp=(0, 0))
                    pt_tok[par] = mm(onesc[:], pt_b, bank(sbk, p0=32, p1=33), start=False,
                                     stop=False, tp=(0, 32))
                    if kt == 0 and qt + 1 < NQT:
                        stage_q(qt + 1)
                q_cons[qp] = s_tok[nk - 1]
                pending = (hp, qt, sbk)
        emit_closing(*pending)
        pending = None

        if dbg and b == 0:
            out_final.append(P.emit("sync", lambda e: e.dma_start(out=dbg_h["OT"], in_=D[:]),
                                    waits=[ot_last], inc="dO", amt=16))

        P.barrier()
        attn_done = [ot_last]
        h_toks = []
        for e in range(8):
            h_toks.append(P.emit("sync",
                                 (lambda e=e: (lambda s: s.dma_start(
                                     out=C[:, e * E:(e + 1) * E],
                                     in_=had_h[e * 128:(e + 1) * 128, :])))(),
                                 waits=attn_done + [P.tick("tT")], inc="dS", amt=16))
        wo_toks = []
        for e in range(8):
            wo_toks.append(P.emit("gpsimd",
                                  (lambda e=e: (lambda g: g.dma_start(
                                      out=Ew[:, e * E:(e + 1) * E],
                                      in_=wo_w[e * 128:(e + 1) * 128, :])))(),
                                  waits=attn_done + [P.tick("tT")], inc="dG", amt=16))
        wq_tok = None
        if b == 1:
            wq_tok = P.emit("sync",
                            lambda e: e.dma_start(out=C[:, 8 * E:16 * E], in_=wqT_d),
                            waits=attn_done + [P.tick("tT")], inc="dS", amt=16)
        for oc in range(8 if b == 0 else 0):
            lt = P.emit("sync",
                        (lambda oc=oc: (lambda s: s.dma_start(
                            out=Fq[:, 0:E], in_=wf_w[oc * 128:(oc + 1) * 128, :])))(),
                        waits=attn_done + [P.tick("tT"), wq_tok], inc="dS", amt=16)
            P.emit("vector", lambda v: v.reduce_sum(stats[:, 25:26], Fq[:, 0:E], AX.X,
                                                    apply_absolute_value=True), waits=[lt])
            P.emit("vector", lambda v: v.drain())
            P.emit("vector", lambda v: v.tensor_scalar(stats[:, 26:27], stats[:, 25:26],
                                                       1.0 / 1024.0, 1e-5, ALU.mult, ALU.max))
            P.emit("vector", lambda v: v.drain())
            P.emit("vector", lambda v: v.reciprocal(stats[:, 27:28], stats[:, 26:27]))
            P.emit("vector", lambda v: v.drain())
            P.emit("vector", lambda v: v.tensor_scalar(Fq[:, E:2 * E], Fq[:, 0:E], stats[:, 27:28], None, ALU.mult))
            P.emit("vector", lambda v: v.tensor_single_scalar(bbufq[:], Fq[:, E:2 * E], 0.5, ALU.is_gt))
            P.emit("vector", lambda v: v.tensor_single_scalar(Fq[:, 0:E], Fq[:, E:2 * E], -0.5, ALU.is_lt))
            P.emit("vector", lambda v: v.tensor_sub(Fq[:, E:2 * E], bbufq[:], Fq[:, 0:E]))
            qt_ = P.emitd("vector", lambda v: v.tensor_scalar(wqtile[:], Fq[:, E:2 * E], stats[:, 26:27],
                                                             None, ALU.mult), inc="tV")
            for half in range(2):
                bk = half
                w0 = [qt_, tokb[bk]]
                mt = None
                for j in range(4):
                    ic = half * 4 + j
                    mt = mm(wqtile[:, ic * 128:(ic + 1) * 128], iden,
                            bank(bk, j * 128, (j + 1) * 128), start=True, stop=True,
                            waits=w0 if j == 0 else ())
                dst = C[:].rearrange("p (c t) -> p c t", c=2 * NP)[:, 8 + half * 4: 8 + (half + 1) * 4,
                                                                  oc * 128:(oc + 1) * 128]
                src = bank(bk).rearrange("p (c t) -> p c t", c=4)
                tokb[bk] = P.emitd("vector",
                                  (lambda dst=dst, src=src: (lambda v: v.tensor_copy(dst, src)))(),
                                  waits=[mt], inc="tV")
                wq_tok = tokb[bk]
        if b == 0:
            wq_tok = P.emit("sync",
                            lambda e: e.dma_start(out=wqT_d, in_=C[:, 8 * E:16 * E]),
                            waits=[wq_tok], inc="dS", amt=16)
        wprep = [wq_tok, h_toks[-1], wo_toks[-1]]
        if dbg and b == 0:
            out_final.append(P.emit("sync", lambda e: e.dma_start(out=dbg_h["wqT"], in_=C[:, 8 * E:16 * E]),
                                    waits=[wq_tok], inc="dO", amt=16))

        P.barrier()
        x2cons = {}
        x2free = {}
        ostore = {}
        x2toks = {}
        xh_toks = {}
        h2T_toks = {}
        ffn_mts = {}
        tp_toks = {}

        def cd_stage1(tt):
            sl = tt % 2
            xa = xbuf[:, sl * E:(sl + 1) * E]
            xl = P.emit("sync",
                        (lambda tt=tt, xa=xa, xb=xb: (lambda e: e.dma_start(
                            out=xa, in_=xb[tt * 128:(tt + 1) * 128, :])))(),
                        waits=[x2cons.get(tt - 2, hnorm[NT - 2 + sl])], inc="dS", amt=16)
            for half in range(2):
                bk = half
                w0 = [tokb[bk], attn_done[0], x2free.get(tt - 2)] + wprep
                mm(onesr[0:1, 0:128], bor[0:1, half * 512:(half + 1) * 512], bank(bk),
                   start=True, stop=False, waits=w0)
                mt = None
                for vc in range(8):
                    mt = mm(D[:, vc * S + tt * 128: vc * S + (tt + 1) * 128],
                            Ew[:, vc * E + half * 512: vc * E + (half + 1) * 512], bank(bk),
                            start=False, stop=(vc == 7))
                tokb[bk] = P.emit("vector",
                                  (lambda bk=bk, sl=sl, half=half: (lambda v: v.tensor_add(
                                      x2buf[:, sl * E + half * 512: sl * E + (half + 1) * 512],
                                      bank(bk),
                                      xbuf[:, sl * E + half * 512: sl * E + (half + 1) * 512])))(),
                                  waits=[mt, xl, x2free.get(tt - 2)], inc="tV")
            x2cons[tt] = tokb[1]
            x2toks[tt] = tokb[1]

        def cd_stage2(tt):
            sl = tt % 2
            h2Tp = h2T[:] if sl == 0 else G[:, 0:1024]
            x2tok = x2toks[tt]
            x2a = x2buf[:, sl * E:(sl + 1) * E]
            if dbg and b == 0:
                out_final.append(P.emit("sync",
                                        (lambda tt=tt, x2a=x2a: (lambda e: e.dma_start(
                                            out=dbg_h["x2"][tt * 128:(tt + 1) * 128, :], in_=x2a)))(),
                                        waits=[x2tok], inc="dO", amt=16))
            ha = hbuf[:, sl * E:(sl + 1) * E]
            P.emit("vector", (lambda x2a=x2a: (lambda v: v.bn_stats(stats[:, 0:6], x2a[:, 0:512])))(), waits=[x2tok])
            P.emit("vector", (lambda x2a=x2a: (lambda v: v.bn_stats(stats[:, 6:12], x2a[:, 512:1024])))())
            P.emit("vector", lambda v: v.drain())
            P.emit("vector", lambda v: v.bn_aggr(stats[:, 12:14], stats[:, 0:12]))
            P.emit("vector", lambda v: v.drain())
            P.emit("vector", lambda v: v.tensor_scalar_mul(stats[:, 14:15], stats[:, 12:13], -1.0))
            vt = P.emit("vector", lambda v: v.tensor_copy(stats[:, 15:16], stats[:, 13:14]), inc="tV")
            st = P.emit("scalar", lambda s: s.activation(stats[:, 16:17], stats[:, 15:16], ACT.Sqrt,
                                                         bias=stats[:, 30:31]),
                        waits=[vt, eps_tok], inc="tS")
            P.emit("vector", lambda v: v.reciprocal(stats[:, 17:18], stats[:, 16:17]), waits=[st])
            P.emit("vector", lambda v: v.drain())
            h2tok = P.emitd("vector",
                           (lambda x2a=x2a, ha=ha: (lambda v: v.tensor_scalar(
                               ha, x2a, stats[:, 14:15], stats[:, 17:18], ALU.add, ALU.mult)))(),
                           inc="tV")
            for half in range(2):
                bk = 2 + half
                w0 = [h2tok, tokb[bk]]
                mt = None
                for j in range(4):
                    e = half * 4 + j
                    mt = mm(ha[:, e * 128:(e + 1) * 128], iden, bank(bk, j * 128, (j + 1) * 128),
                            start=True, stop=True, waits=w0 if j == 0 else ())
                tokb[bk] = P.emitd("vector",
                                  (lambda bk=bk, half=half, h2Tp=h2Tp: (lambda v: v.tensor_copy(
                                      h2Tp[:, half * 512:(half + 1) * 512], bank(bk))))(),
                                  waits=[mt], inc="tV")
            h2T_toks[tt] = tokb[3]

        def cd_stage2b(tt):
            sl = tt % 2
            h2Tp = h2T[:] if sl == 0 else G[:, 0:1024]
            for half in range(2):
                bk = 4 + half
                w0 = [tokb[bk], h2T_toks[tt], tokb[2]]
                for e in range(8):
                    mm(h2Tp[:, e * 128:(e + 1) * 128],
                       C[:, e * E + half * 512: e * E + (half + 1) * 512], bank(bk),
                       start=(e == 0), stop=(e == 7), waits=w0 if e == 0 else ())
            xh_toks[tt] = P.tick("tT")

        def cd_stage3(tt):
            sl = tt % 2
            xh_tok = xh_toks[tt]
            P.emit("vector", lambda v: v.reduce_max(stats[:, 20:21], PS[:, 4 * 512:6 * 512], AX.X,
                                                    apply_absolute_value=True), waits=[xh_tok])
            P.emit("vector", lambda v: v.drain())
            P.emit("vector", lambda v: v.tensor_scalar_max(stats[:, 21:22], stats[:, 20:21], 1e-5))
            P.emit("vector", lambda v: v.drain())
            P.emit("vector", lambda v: v.reciprocal(stats[:, 22:23], stats[:, 21:22]))
            P.emit("vector", lambda v: v.drain())
            P.emit("vector", lambda v: v.tensor_scalar_mul(stats[:, 23:24], stats[:, 22:23], 7.0))
            P.emit("vector", lambda v: v.tensor_scalar_mul(stats[:, 24:25], stats[:, 21:22], 1.0 / 7.0))
            P.emit("vector", lambda v: v.drain())
            Tbuf = Fq[:, 0:E]
            tp_ = P.emit("vector",
                         (lambda Tbuf=Tbuf: (lambda v: v.tensor_scalar(
                             Tbuf, PS[:, 4 * 512:6 * 512], stats[:, 23:24], 8.5, ALU.mult, ALU.add)))(),
                         inc="tV")
            tokb[4] = tp_
            tokb[5] = tp_
            tp_toks[tt] = tp_

        def cd_stage3b(tt):
            sl = tt % 2
            qibufp = qibuf[:] if sl == 0 else KBs[:, 0:1024]
            qiTp = qiT[:] if sl == 0 else G[:, 1024:2048]
            Tbuf = Fq[:, 0:E]
            r_cur = Fq[:, E:2 * E]
            bbuf = bbufq[:]
            first = True
            for c in (8.0, 4.0, 2.0, 1.0):
                srcb = Tbuf if first else r_cur
                P.emit("vector", (lambda srcb=srcb, c=c: (lambda v: v.tensor_scalar(
                    bbuf, srcb, c, c, ALU.is_ge, ALU.mult)))())
                P.emit("vector", (lambda srcb=srcb, r_cur=r_cur: (lambda v: v.tensor_sub(
                    r_cur, srcb, bbuf)))())
                first = False
            qitok = P.emitd("vector",
                           (lambda Tbuf=Tbuf, r_cur=r_cur, qibufp=qibufp: (lambda v: v.scalar_tensor_tensor(
                               qibufp, Tbuf, 8.0, r_cur, ALU.subtract, ALU.subtract)))(),
                           inc="tV")
            for half in range(2):
                bk = 2 + half
                w0 = [qitok, tokb[bk]]
                mt = None
                for j in range(4):
                    e = half * 4 + j
                    mt = mm(qibufp[:, e * 128:(e + 1) * 128], iden, bank(bk, j * 128, (j + 1) * 128),
                            start=True, stop=True, waits=w0 if j == 0 else ())
                tokb[bk] = P.emitd("vector",
                                  (lambda bk=bk, half=half, qiTp=qiTp: (lambda v: v.tensor_copy(
                                      qiTp[:, half * 512:(half + 1) * 512], bank(bk))))(),
                                  waits=[mt], inc="tV")
            qiT_tok = tokb[3]
            ffn_mt = {}
            for half in range(2):
                bk = 6 + half
                w0 = [tokb[bk], qiT_tok]
                mt = None
                for ic in range(8):
                    mt = mm(qiTp[:, ic * 128:(ic + 1) * 128],
                            C[:, 8 * E + ic * E + half * 512: 8 * E + ic * E + (half + 1) * 512],
                            bank(bk), start=(ic == 0), stop=(ic == 7), waits=w0 if ic == 0 else ())
                ffn_mt[half] = mt
            ffn_mts[tt] = ffn_mt

        def cd_finish(tt):
            sl = tt % 2
            ffn_mt = ffn_mts[tt]
            for half in range(2):
                bk = 6 + half
                tokb[bk] = P.emitd("vector",
                                  (lambda bk=bk, sl=sl, half=half: (lambda v: v.scalar_tensor_tensor(
                                      obuf[:, half * 512:(half + 1) * 512],
                                      bank(bk), stats[:, 28:29] if False else stats[:, 24 + (tt % 2) * 0:25],
                                      x2buf[:, sl * E + half * 512: sl * E + (half + 1) * 512],
                                      ALU.mult, ALU.add)))(),
                                  waits=[ffn_mt[half], ostore.get(tt - 1)], inc="tV")
            ostore[tt] = P.emit("sync",
                                (lambda tt=tt, ob=ob: (lambda e: e.dma_start(
                                    out=ob[tt * 128:(tt + 1) * 128, :],
                                    in_=obuf[:])))(),
                                waits=[tokb[7]], inc="dO", amt=16)
            out_final.append(ostore[tt])
            x2free[tt] = tokb[7]

        cd_stage1(0)
        cd_stage2(0)
        cd_stage2b(0)
        for tt in range(NT):
            if tt + 1 < NT:
                cd_stage1(tt + 1)
            cd_stage3(tt)
            cd_stage3b(tt)
            if tt + 1 < NT:
                cd_stage2(tt + 1)
            cd_finish(tt)
            if tt + 1 < NT:
                cd_stage2b(tt + 1)

    P.emit("sync", lambda e: e.nop(), waits=[P.tick("dO")])

    with nc.Block() as block:
        es = P.build(nc, block)
    nc._keepalive = es
    return nc


_BUILD_CACHE = {}


def get_nc(dbg=False):
    if dbg not in _BUILD_CACHE:
        _BUILD_CACHE[dbg] = build_nc(dbg)
    return _BUILD_CACHE[dbg]


def make_inmaps(inputs, dbg=False):
    x = np.ascontiguousarray(np.asarray(inputs["L0_attn_x"], np.float32))
    consts = host_consts()
    bq = np.asarray(inputs["L0_attn_bq"], np.float32)
    bkv = np.asarray(inputs["L0_attn_bkv"], np.float32)
    bo = np.asarray(inputs["L0_attn_bo"], np.float32)
    common = {
        "wq_w": np.ascontiguousarray(np.asarray(inputs["L0_attn_Wq"], np.float32)),
        "wkv_w": np.ascontiguousarray(np.asarray(inputs["L0_attn_Wkv"], np.float32)),
        "wo_w": np.ascontiguousarray(np.asarray(inputs["L0_attn_Wo"], np.float32)),
        "wf_w": np.ascontiguousarray(np.asarray(inputs["L0_ffn_weight"], np.float32)),
        "bq_t": np.ascontiguousarray(bq.reshape(8, 128).T),
        "bk_t": np.ascontiguousarray(bkv[:E].reshape(8, 128).T),
        "bv_row": np.ascontiguousarray(bkv[E:].reshape(1, E)),
        "bo_row": np.ascontiguousarray(bo.reshape(1, E)),
        "cos_t": consts["cos_t"], "sin_t": consts["sin_t"],
        "had": consts["had"], "iden": consts["iden"], "perm": consts["perm"],
        "perm2": consts["perm2"], "sel": consts["sel"],
        "ones512": consts["ones512"], "onescol": consts["onescol"],
        "epsrow": consts["epsrow"], "ltri": consts["ltri"],
    }
    in_maps = []
    for c in range(8):
        m = dict(common)
        m["x_in"] = np.ascontiguousarray(x[2 * c:2 * c + 2])
        in_maps.append(m)
    return in_maps


def kernel(**inputs):
    nc = get_nc(dbg=False)
    in_maps = make_inmaps(inputs)
    res = run_bass_kernel_spmd(nc, in_maps, core_ids=list(range(8)))
    out = np.concatenate([np.asarray(r["out"]) for r in res.results], axis=0)
    return out.astype(np.float32)
